# version 05 (final)
# baseline (speedup 1.0000x reference)
import math
import numpy as np
import ml_dtypes
from contextlib import ExitStack

import concourse.bass as bass
import concourse.mybir as mybir
import concourse.bass_isa as bass_isa
from concourse.bass_utils import run_bass_kernel_spmd

F32 = mybir.dt.float32
BF16 = mybir.dt.bfloat16
AX = mybir.AxisListType
ALU = mybir.AluOpType
ACT = mybir.ActivationFunctionType
ROP = bass_isa.ReduceOp

S = 2048
T = 2 * S
D = 2048
DC = D // 128
NH = 16
H0 = 8192
H0C = H0 // 128
H1 = 5632
H1C = H1 // 128
ISQ = 1.0 / math.sqrt(128.0)
EPS_LN = 1e-5

ENGINES = ("tensor", "vector", "scalar", "gpsimd", "sync")
T_, V_, S_, G_, SY = ENGINES

PHASES = ("A", "B", "C", "D")
DEBUG_TAP = None
A_LIMIT = None


class Chan:
    __slots__ = ("name", "sem", "step", "count")

    def __init__(self, name, sem, step):
        self.name, self.sem, self.step, self.count = name, sem, step, 0

    def bump(self):
        self.count += self.step
        return self.count


class DmaQ:

    def __init__(self, rec, name, eng, nlanes=8):
        self.eng = eng
        self.lanes = [rec.chan(f"{name}{i}", 16) for i in range(nlanes)]
        self.rr = 0

    def next_lane(self):
        lane = self.lanes[self.rr % len(self.lanes)]
        self.rr += 1
        return lane


class Buf:
    __slots__ = ("name", "writer", "readers")

    def __init__(self, name):
        self.name = name
        self.writer = None
        self.readers = {}


class Rec:
    def __init__(self, nc, stack):
        self.nc = nc
        self.stack = stack
        self.ops = {e: [] for e in ENGINES}
        self.marks = {e: {} for e in ENGINES}
        self.chans = []
        self.ch = {}
        for e in (T_, V_, S_, G_):
            self.ch[e] = self.chan("c_" + e, 1)

    def chan(self, name, step):
        sem = self.stack.enter_context(self.nc.semaphore(name))
        c = Chan(name, sem, step)
        self.ch[name] = c
        self.chans.append(c)
        return c

    def _need(self, eng, waits, chan, cnt):
        if chan is None or cnt <= 0:
            return
        if cnt > self.marks[eng].get(chan.name, 0):
            waits.append((chan, cnt))
            self.marks[eng][chan.name] = cnt

    def _gather(self, eng, reads, writes):
        waits = []
        for b in reads:
            if b is not None and b.writer is not None:
                self._need(eng, waits, *b.writer)
        for b in writes:
            if b is not None:
                if b.writer is not None:
                    self._need(eng, waits, *b.writer)
                for (c, v) in b.readers.values():
                    self._need(eng, waits, c, v)
        return waits

    def _commit(self, reads, writes, chan, tok):
        for b in writes:
            if b is not None:
                b.writer = (chan, tok)
                b.readers = {}
        for b in reads:
            if b is not None:
                b.readers[chan.name] = (chan, tok)

    def emit(self, eng, fn, reads=(), writes=(), chan=None, dbg="", extra=()):
        waits = self._gather(eng, reads, writes)
        for (c, v) in extra:
            self._need(eng, waits, c, v)
        if chan is None:
            chan = self.ch[eng]
        tok = chan.bump()
        self.ops[eng].append((waits, fn, chan, dbg))
        self._commit(reads, writes, chan, tok)
        return tok

    def emit_group(self, eng, items, writes=(), dbg=""):
        all_reads = []
        for fn, reads in items:
            all_reads.extend(reads)
        waits = self._gather(eng, all_reads, writes)
        chan = self.ch[eng]
        tok = chan.bump()
        n = len(items)
        for i, (fn, reads) in enumerate(items):
            self.ops[eng].append((waits if i == 0 else [], fn,
                                  chan if i == n - 1 else None, dbg))
        self._commit(all_reads, writes, chan, tok)
        return tok

    def barrier(self, dbg="bar"):
        for eng in ENGINES:
            waits = []
            for c in self.chans:
                self._need(eng, waits, c, c.count)
            if waits:
                self.ops[eng].append((waits, None, None, dbg))

    def replay(self):
        with self.nc.allow_low_precision(reason=(
                "deliberate bf16 staging of scales/activations; end-to-end "
                "accuracy validated against the reference")), \
                self.nc.Block() as block:
            for eng in ENGINES:
                seq = self.ops[eng]
                selfwait = (eng == G_)
                if selfwait:
                    def body_g(e, seq=seq):
                        running = {}
                        for waits, fn, chan, dbg in seq:
                            for (c, v) in waits:
                                e.wait_ge(c.sem, v)
                            if fn is not None:
                                inst = fn(e)
                                if chan is not None and inst is not None:
                                    tok = running.get(chan.name, 0) + chan.step
                                    running[chan.name] = tok
                                    inst.then_inc(chan.sem, chan.step)
                                    if chan.step == 1:
                                        e.wait_ge(chan.sem, tok)
                    getattr(block, eng)(body_g)
                else:
                    def body_o(e, seq=seq):
                        for waits, fn, chan, dbg in seq:
                            for (c, v) in waits:
                                e.wait_ge(c.sem, v)
                            if fn is not None:
                                inst = fn(e)
                                if chan is not None and inst is not None:
                                    inst.then_inc(chan.sem, chan.step)
                    getattr(block, eng)(body_o)


def build_program(phases=PHASES, debug_tap=DEBUG_TAP):
    nc = bass.Bass("TRN2", target_bir_lowering=False)
    stack = ExitStack()
    rec = Rec(nc, stack)

    def din(name, shape, dtype=F32):
        return nc.dram_tensor(name, list(shape), dtype, kind="ExternalInput").ap()

    xT = din("xT", (D, T))
    wqT = din("wqT", (D, D)); wkT = din("wkT", (D, D))
    wvT = din("wvT", (D, D)); woT = din("woT", (D, D))
    gwT = din("gwT", (D, H0)); vwT = din("vwT", (D, H0)); owT = din("owT", (H0, D))
    wqkv = din("wqkv", (D, 2304)); wfc = din("wfc", (D, D))
    w1T = din("w1T", (D, H1)); w2T = din("w2T", (D, H1)); w3T = din("w3T", (H1, D))
    gbH = din("gbH", (128, H0C)); vbH = din("vbH", (128, H0C))
    obH = din("obH", (128, DC)); qkvbH = din("qkvbH", (128, 18)); fcbH = din("fcbH", (128, DC))
    cosH = din("cosH", (128, S), BF16); sinH = din("sinH", (128, S), BF16)
    identH = din("identH", (128, 128), BF16)
    onesH = din("onesH", (128, 128), BF16)
    rotH = din("rotH", (128, 128), BF16)
    maskH = din("maskH", (128, 128))

    yT = nc.dram_tensor("yT", [D, T], F32, kind="ExternalOutput").ap()

    def dscr(name, shape, dtype):
        return nc.dram_tensor(name, list(shape), dtype, kind="Internal").ap()

    resid1 = dscr("resid1", (D, T), F32)
    resid2 = dscr("resid2", (D, T), F32)
    resid3 = dscr("resid3", (D, T), F32)
    y_scr = dscr("y_scr", (D, T), BF16)
    q_hbm = dscr("q_hbm", (D, S), BF16)
    v_hbm = dscr("v_hbm", (D, S), BF16)
    ao_hbm = dscr("ao_hbm", (D, S), BF16)
    hmul_hbm = dscr("hmul_hbm", (H0, T), BF16)
    h1_hbm = dscr("h1_hbm", (H1, T), BF16)
    town = dscr("town", (H0C * 8, 128 * 256), BF16)
    tw3 = dscr("tw3", (H1C * 8, 128 * 256), BF16)
    srow_hbm = dscr("srow_hbm", (1, T), BF16)
    gsorow_hbm = dscr("gsorow_hbm", (1, T), BF16)
    brow_hbm = dscr("brow_hbm", (3, T), BF16)

    hbuf = {}

    def hb(ap_):
        k = ap_.tensor.name
        if k not in hbuf:
            hbuf[k] = Buf(k)
        return hbuf[k]

    def sb(name, shape, dtype, scope=None):
        t = (scope or stack).enter_context(nc.sbuf_tensor(name, list(shape), dtype))
        return t.ap(), Buf(name)

    CB = stack.enter_context(nc.sbuf_tensor("CB", [128, 1024], BF16)).ap()
    ident, b_ident = CB[:, 0:128], Buf("ident")
    ones, b_ones = CB[:, 128:256], Buf("ones")
    rotm, b_rotm = CB[:, 256:384], Buf("rotm")
    tcol, b_tcol = CB[:, 384:400], Buf("tcol")
    bgc, b_bgc = CB[:, 400:401], Buf("bgc")
    rowm, b_rowm = CB[0:16, 512:640], Buf("rowm")
    rowr, b_rowr = CB[0:16, 640:768], Buf("rowr")
    rowa, b_rowa = CB[0:16, 768:896], Buf("rowa")
    rowb, b_rowb = CB[0:16, 896:1024], Buf("rowb")
    CF = stack.enter_context(nc.sbuf_tensor("CF", [128, 352], F32)).ap()
    maskt, b_maskt = CF[:, 0:128], Buf("maskt")
    gbc, b_gbc = CF[:, 128:192], Buf("gbc")
    vbc, b_vbc = CF[:, 192:256], Buf("vbc")
    obc, b_obc = CF[:, 256:272], Buf("obc")
    qkvbc, b_qkvbc = CF[:, 272:290], Buf("qkvbc")
    fcbc, b_fcbc = CF[:, 290:306], Buf("fcbc")
    gacc, b_gacc = CF[:, 306:314], Buf("gacc")
    gsc, b_gsc = CF[:, 314:326], Buf("gsc")
    colw, b_colw = CF[:, 326:334], Buf("colw")
    epscol, b_eps = CF[:, 334:335], Buf("epscol")
    zcol, b_zc = CF[:, 335:336], Buf("zcol")
    m20col, b_m20 = CF[:, 336:337], Buf("m20col")
    AA = stack.enter_context(nc.sbuf_tensor("AA", [128, 32768], BF16)).ap()
    bAA = [Buf(f"AA{k}") for k in range(16)]
    AB = stack.enter_context(nc.sbuf_tensor("AB", [128, 32768], BF16)).ap()
    bAB = [Buf(f"AB{k}") for k in range(16)]
    TB, tb_b = [], []
    for i in range(8):
        _a, _b = sb(f"TB{i}", (128, 2048), BF16)
        TB.append(_a); tb_b.append(_b)
    TF, tf_b = [], []
    for i in range(4):
        _a, _b = sb(f"TF{i}", (128, 2048), F32)
        TF.append(_a); tf_b.append(_b)
    WF0, b_WF0 = sb("WF0", (128, 2048), F32)
    WFp = [(WF0, b_WF0), (TF[0], tf_b[0])]

    psPt = stack.enter_context(nc.psum_tensor("psP", [128, 2048], F32))
    psQt = stack.enter_context(nc.psum_tensor("psQ", [128, 2048], F32))
    psP, psQ = psPt.ap(), psQt.ap()
    _bP, _bQ = Buf("psP"), Buf("psQ")
    b_psP = [_bP] * 4
    b_psQ = [_bQ] * 4

    ch_ls = DmaQ(rec, "d_ls", SY, 8)
    ch_la = DmaQ(rec, "d_la", S_, 4)
    ch_lg = DmaQ(rec, "d_lg", G_, 8)
    ch_sg = DmaQ(rec, "d_sg", G_, 8)

    def dma(qeng, q, out_ap, in_ap, reads, writes, dbg=""):
        lane = q.next_lane()
        extra = [(lane, lane.count)] if lane.count else []
        rec.emit(q.eng, lambda e, o=out_ap, i=in_ap: e.dma_start(out=o, in_=i),
                 reads=reads, writes=writes, chan=lane, dbg=dbg, extra=extra)

    rec.emit(V_, lambda e: e.memset(epscol[:], EPS_LN), reads=[], writes=[b_eps])
    rec.emit(V_, lambda e: e.memset(zcol[:], 0.0), reads=[], writes=[b_zc])
    rec.emit(V_, lambda e: e.memset(m20col[:], -20.0), reads=[], writes=[b_m20])
    rec.emit(V_, lambda e: e.memset(gacc[:], 0.0), reads=[], writes=[b_gacc])
    rec.emit(V_, lambda e: e.memset(gsc[:], 0.0), reads=[], writes=[b_gsc])
    rec.emit(V_, lambda e: e.memset(colw[:], 0.0), reads=[], writes=[b_colw])

    for dst, bb, csrc in ((ident, b_ident, identH), (ones, b_ones, onesH),
                          (rotm, b_rotm, rotH), (maskt, b_maskt, maskH),
                          (gbc, b_gbc, gbH), (vbc, b_vbc, vbH), (obc, b_obc, obH),
                          (qkvbc, b_qkvbc, qkvbH), (fcbc, b_fcbc, fcbH)):
        dma(SY, ch_ls, dst[:], csrc[:], [hb(csrc)], [bb])

    GAMS = [("gw", gwT, "l0"), ("vw", vwT, "l0"), ("ow", owT, "l0"),
            ("w1", w1T, "l1"), ("w2", w2T, "l1"), ("w3", w3T, "l1")]
    GIDX = {n: i for i, (n, _, _) in enumerate(GAMS)}

    def gamma_steps():
        for gi, (name, w_ap, kind) in enumerate(GAMS):
            rows, cols = w_ap.shape
            wv = w_ap.rearrange("(n p) c -> p n c", p=128)
            nrow = wv.shape[1]
            chunks = []
            for ni in range(nrow):
                for off in range(0, cols, 2048):
                    chunks.append((ni, off, min(2048, cols - off)))
            for j, (ni, off, wdt) in enumerate(chunks):
                slot = j % 2
                def _step(j=j, slot=slot, gi=gi, wv=wv, w_ap=w_ap,
                          ni=ni, off=off, wdt=wdt):
                    wfa, wfbuf = WFp[slot]
                    dma(SY, ch_ls, wfa[:, 0:wdt],
                        wv[:, ni, off:off + wdt], [hb(w_ap)], [wfbuf],
                        dbg=f"gld{gi}.{j}")
                    rec.emit(V_, lambda e, wfa=wfa, wdt=wdt: e.tensor_reduce(
                        colw[:, 4:5], wfa[:, 0:wdt], AX.X,
                        ALU.add, apply_absolute_value=True),
                        reads=[wfbuf], writes=[b_colw], dbg=f"gred{gi}.{j}")
                    if j == 0:
                        rec.emit(V_, lambda e, gi=gi: e.tensor_copy(
                            gacc[:, gi:gi + 1], colw[:, 4:5]),
                            reads=[b_colw], writes=[b_gacc])
                    else:
                        rec.emit(V_, lambda e, gi=gi: e.tensor_add(
                            gacc[:, gi:gi + 1], gacc[:, gi:gi + 1], colw[:, 4:5]),
                            reads=[b_colw], writes=[b_gacc])
                yield _step
            total = rows * cols // 128
            def _fin(gi=gi, kind=kind, total=total):
                nelem = float(total * 128)
                rec.emit(V_, lambda e, gi=gi: e.tensor_copy(
                    bgc[:, 0:1], gacc[:, gi:gi + 1]),
                    reads=[b_gacc], writes=[b_bgc], dbg=f"gcast{gi}")
                rec.emit_group(T_, [(lambda e: e.matmul(
                    psP[:, 0:1], ones[:], bgc[:, 0:1], start=True, stop=True),
                    [b_ones, b_bgc])], writes=b_psP, dbg=f"gar{gi}")
                rec.emit(S_, lambda e, n=nelem: e.activation(
                    colw[:, 5:6], psP[:, 0:1], ACT.Identity, bias=zcol,
                    scale=1.0 / n),
                    reads=b_psP + [b_zc], writes=[b_colw], dbg=f"gev{gi}")
                if kind == "l0":
                    rec.emit(V_, lambda e: e.tensor_scalar_max(
                        colw[:, 5:6], colw[:, 5:6], 1e-5),
                        reads=[], writes=[b_colw])
                else:
                    rec.emit(V_, lambda e: e.tensor_scalar_add(
                        colw[:, 5:6], colw[:, 5:6], 1e-8),
                        reads=[], writes=[b_colw])
                rec.emit(V_, lambda e, gi=gi: e.tensor_scalar_mul(
                    gsc[:, 2 * gi:2 * gi + 1], colw[:, 5:6], 0.5),
                    reads=[b_colw], writes=[b_gsc])
                rec.emit(V_, lambda e, gi=gi: e.tensor_scalar_mul(
                    gsc[:, 2 * gi + 1:2 * gi + 2], colw[:, 5:6], -0.5),
                    reads=[b_colw], writes=[b_gsc])
            yield _fin

    _gamma_iter = gamma_steps()
    _gamma_done = [False]

    def pump_gamma(n=1):
        if _gamma_done[0]:
            return
        for _ in range(n):
            try:
                next(_gamma_iter)()
            except StopIteration:
                _gamma_done[0] = True
                return

    def finish_all_gamma():
        while not _gamma_done[0]:
            pump_gamma(8)

    def proj_group(ps_ap, ps_bufs, wsl, b_wsl, rhs_slab, b_rhs, width, nks, dbg=""):
        items = []
        nb = width // 512
        for k in range(nks):
            for j in range(nb):
                items.append((
                    lambda e, k=k, j=j: e.matmul(
                        ps_ap[:, j * 512:(j + 1) * 512],
                        wsl[:, k * 128:(k + 1) * 128],
                        rhs_slab(k)[:, j * 512:(j + 1) * 512],
                        start=(k == 0), stop=(k == nks - 1)),
                    [b_wsl, b_rhs[k] if isinstance(b_rhs, list) else b_rhs]))
        rec.emit_group(T_, items, writes=ps_bufs, dbg=dbg)

    def tern_tile(dst, b_dst, src, b_src, lt, b_lt, gi, width, dbg=""):
        rec.emit(V_, lambda e: e.tensor_single_scalar(
            lt[:, 0:width], src, gsc[:, 2 * gi + 1:2 * gi + 2], ALU.is_lt),
            reads=[b_src, b_gsc], writes=[b_lt], dbg=dbg + "lt")
        rec.emit(V_, lambda e: e.scalar_tensor_tensor(
            dst, src, gsc[:, 2 * gi:2 * gi + 1], lt[:, 0:width],
            ALU.is_gt, ALU.subtract),
            reads=[b_src, b_gsc, b_lt], writes=[b_dst], dbg=dbg + "gt")

    psPb = psP.bitcast(BF16)

    def pmax_rows(foldb, b_fold, xt, b_xt, row, b_row, dbg=""):
        for g4 in range(4):
            base = (g4 % 2) * 1024
            pq = psPb[:, base:base + 512]
            pb = [b_psP[g4 % 2]]
            def _tp(e, kk, pq=pq, g4=g4):
                with nc.allow_low_precision(reason="bf16 transpose move"):
                    return e.transpose(
                        pq[:, kk * 128:(kk + 1) * 128],
                        foldb[:, (g4 * 4 + kk) * 128:(g4 * 4 + kk + 1) * 128],
                        ident[:])
            items = [(lambda e, kk=kk, f=_tp: f(e, kk), [b_fold, b_ident])
                     for kk in range(4)]
            rec.emit_group(T_, items, writes=pb, dbg=dbg + f"tp{g4}")
            rec.emit(S_, lambda e, g4=g4, pq=pq: e.activation(
                xt[:, g4 * 512:(g4 + 1) * 512], pq, ACT.Identity, bias=zcol),
                reads=pb + [b_zc], writes=[b_xt], dbg=dbg + f"tev{g4}")
        for i in range(16):
            rec.emit(V_, lambda e, i=i: e.tensor_reduce(
                tcol[:, i:i + 1], xt[:, i * 128:(i + 1) * 128], AX.X, ALU.max),
                reads=[b_xt], writes=[b_tcol], dbg=dbg + f"red{i}")
        def _tpr(e):
            with nc.allow_low_precision(reason="bf16 transpose move"):
                return e.transpose(psPb[0:16, 0:128], tcol[:, 0:16], ident[:])
        rec.emit_group(T_, [(_tpr, [b_tcol, b_ident])], writes=b_psP,
                       dbg=dbg + "tpr")
        rec.emit(S_, lambda e: e.activation(
            row, psPb[0:16, 0:128], ACT.Identity, bias=zcol[0:16, :]),
            reads=b_psP + [b_zc], writes=[b_row], dbg=dbg + "rev")
        rec.emit(V_, lambda e: e.tensor_scalar_max(row, row, 1e-5),
                 reads=[], writes=[b_row], dbg=dbg + "grd")

    def bcast_rows(row, b_row, dst32, b_dst, slot, tg0, dbg=""):
        dma(G_, ch_sg,
            brow_hbm[slot:slot + 1, tg0:tg0 + S].rearrange("o (r c) -> (o r) c", r=16),
            row, [b_row], [hb(brow_hbm)], dbg=dbg + "st")
        dma(G_, ch_lg, dst32,
            brow_hbm[slot:slot + 1, tg0:tg0 + S].broadcast_to((128, S)),
            [hb(brow_hbm)],
            b_dst if isinstance(b_dst, list) else [b_dst], dbg=dbg + "ld")

    def wsl_load(wsl_ap, b_wsl, wT, oc, dbg=""):
        dma(G_, ch_lg, wsl_ap,
            wT[:, oc * 128:(oc + 1) * 128].rearrange("(n p) c -> p n c", p=128),
            [hb(wT)], [b_wsl], dbg=dbg)

    LN = dict(
        ystage=[(TB[5][:, 0:1024], tb_b[5]), (TB[6][:, 0:1024], tb_b[6])],
        sqst=[(TB[7][:, 0:1024], tb_b[7])] * 2,
        rxs=[(TF[2][:, 0:1024], tf_b[2]), (TF[2][:, 1024:2048], tf_b[2])],
        nrm=(TF[1][:, 1024:2048], tf_b[1]),
        mean=(TF[3][:, 0:1024], tf_b[3]),
        inv=(TF[3][:, 1024:2048], tf_b[3]),
        yre=[(TB[7][:, 1024:2048], tb_b[7])] * 2,
    )

    def ln_fold(L, oc, first, last, u0, dbg=""):
        yst, b_yst = L["ystage"][oc % 2]
        sq, b_sq = L["sqst"][oc % 2]
        rec.emit(S_, lambda e, sq=sq, yst=yst: e.activation(sq, yst, ACT.Square),
                 reads=[b_yst], writes=[b_sq], dbg=dbg + "sq")
        items = []
        for j in range(2):
            items.append((lambda e, j=j: e.matmul(
                psQ[:, j * 512:(j + 1) * 512], ones[:],
                yst[:, j * 512:(j + 1) * 512], start=first, stop=last),
                [b_ones, b_yst]))
        for j in range(2):
            items.append((lambda e, j=j: e.matmul(
                psQ[:, 1024 + j * 512:1024 + (j + 1) * 512], ones[:],
                sq[:, j * 512:(j + 1) * 512], start=first, stop=last),
                [b_ones, b_sq]))
        rec.emit_group(T_, items, writes=b_psQ, dbg=dbg + "st")

    def ln_finish(L, y_src_hbm, res_out_hbm, u0, out_f32, dbg=""):
        mean, b_mean = L["mean"]
        inv, b_inv = L["inv"]
        nrm, b_nrm = L["nrm"]
        rec.emit(S_, lambda e: e.activation(mean, psQ[:, 0:1024], ACT.Identity,
                                            bias=zcol, scale=1.0 / D),
                 reads=b_psQ[0:2] + [b_zc], writes=[b_mean], dbg=dbg + "mean")
        rec.emit(S_, lambda e: e.activation(inv, psQ[:, 1024:2048], ACT.Identity,
                                            bias=zcol, scale=1.0 / D),
                 reads=b_psQ[2:4] + [b_zc], writes=[b_inv])
        rec.emit(V_, lambda e: e.tensor_mul(nrm, mean, mean),
                 reads=[b_mean], writes=[b_nrm])
        rec.emit(V_, lambda e: e.tensor_sub(inv, inv, nrm),
                 reads=[b_nrm], writes=[b_inv])
        rec.emit(S_, lambda e: e.activation(inv, inv, ACT.Sqrt,
                                            bias=epscol[:, 0:1]),
                 reads=[b_eps], writes=[b_inv], dbg=dbg + "sqrt")
        rec.emit(V_, lambda e: e.reciprocal(inv, inv),
                 reads=[], writes=[b_inv], dbg=dbg + "rcp")
        for oc in range(DC):
            yre, b_yre = L["yre"][oc % 2]
            dma(S_, ch_la, yre, y_src_hbm[oc * 128:(oc + 1) * 128, u0:u0 + 1024],
                [hb(y_src_hbm)], [b_yre], dbg=dbg + f"yre{oc}")
            rec.emit(V_, lambda e, yre=yre: e.tensor_sub(nrm, yre, mean),
                     reads=[b_yre, b_mean], writes=[b_nrm], dbg=dbg + "sub")
            rec.emit(V_, lambda e: e.tensor_mul(nrm, nrm, inv),
                     reads=[b_inv], writes=[b_nrm])
            dma(G_, ch_sg, res_out_hbm[oc * 128:(oc + 1) * 128, u0:u0 + 1024],
                nrm, [b_nrm], [hb(res_out_hbm)], dbg=dbg + f"rst{oc}")

    def proj_res_ln(tag, wT, bias_cols, src_slab, b_src, res_in, res_out,
                    tg0, L, wsl2, dbg=""):
        for th in range(2):
            u0 = tg0 + th * 1024
            for oc in range(DC):
                wsl, b_w = wsl2[oc % 2]
                wsl_load(wsl, b_w, wT, oc, dbg=dbg + f"w{oc}")
                ph_half = psP[:, (oc % 2) * 1024:(oc % 2) * 1024 + 1024]
                pbufs = b_psP[(oc % 2) * 2:(oc % 2) * 2 + 2]
                items = []
                for k in range(DC):
                    for j in range(2):
                        items.append((lambda e, k=k, j=j, w=wsl, p=ph_half, th=th:
                                      e.matmul(
                            p[:, j * 512:(j + 1) * 512], w[:, k * 128:(k + 1) * 128],
                            src_slab(k)[:, th * 1024 + j * 512:th * 1024 + (j + 1) * 512],
                            start=(k == 0), stop=(k == DC - 1)),
                            [b_w, b_src[k] if isinstance(b_src, list) else b_src]))
                rec.emit_group(T_, items, writes=pbufs, dbg=dbg + f"mm{oc}")
                rx, b_rx = L["rxs"][oc % 2]
                dma(SY, ch_ls, rx, res_in[oc * 128:(oc + 1) * 128, u0:u0 + 1024],
                    [hb(res_in)], [b_rx], dbg=dbg + f"rx{oc}")
                yst, b_yst = L["ystage"][oc % 2]
                bc = bias_cols(oc) if bias_cols is not None else None
                bcol, b_bcol = bc if bc is not None else (zcol, b_zc)
                rec.emit(S_, lambda e, p=ph_half, y=yst, bcol=bcol:
                         e.activation(y, p, ACT.Identity, bias=bcol),
                         reads=pbufs + [b_bcol], writes=[b_yst], dbg=dbg + "res0")
                rec.emit(V_, lambda e, y=yst, r=rx: e.tensor_add(y, y, r),
                         reads=[b_rx], writes=[b_yst], dbg=dbg + "res")
                dma(G_, ch_sg, y_scr[oc * 128:(oc + 1) * 128, u0:u0 + 1024],
                    yst, [b_yst], [hb(y_scr)], dbg=dbg + f"yst{oc}")
                ln_fold(L, oc, oc == 0, oc == DC - 1, u0, dbg=dbg)
            ln_finish(L, y_scr, res_out, u0, True, dbg=dbg + f"ln{th}")

    def phase_a():
        if A_LIMIT == "none":
            rec.barrier("endA")
            return
        slabX, bX = AA, bAA
        slabK, bK = AB, bAB
        wsl2 = [(TB[0], tb_b[0]), (TB[1], tb_b[1])]
        qh, b_qh = TB[2], tb_b[2]
        vf, b_vf = TB[3], tb_b[3]
        vtok, b_vtok = TB[4], tb_b[4]
        t1f, b_t1 = TB[5], tb_b[5]
        t2f, b_t2 = TB[6], tb_b[6]
        qout, b_qout = qh, b_qh
        qs, b_qs = vf, b_vf
        L = LN

        if A_LIMIT == "gamma":
            finish_all_gamma()
            rec.barrier("endA")
            return
        for seq in range(2):
            tok0 = seq * S
            if A_LIMIT == "p1" and seq > 0:
                break
            cosb, b_cos = TB[4], tb_b[4]
            sinb, b_sin = TB[7], tb_b[7]
            dma(SY, ch_ls, cosb[:], cosH[:], [hb(cosH)], [b_cos], dbg="cosld")
            dma(SY, ch_ls, sinb[:], sinH[:], [hb(sinH)], [b_sin], dbg="sinld")
            for k in range(DC):
                dma(G_, ch_lg, slabX[:, k * S:(k + 1) * S],
                    xT[k * 128:(k + 1) * 128, tok0:tok0 + S], [hb(xT)], [bX[k]],
                    dbg=f"xld{k}")
            for proj, wT in (("q", wqT), ("k", wkT), ("v", wvT)):
                for oc in range(DC):
                    pump_gamma(2)
                    wsl, b_w = wsl2[oc % 2]
                    wsl_load(wsl, b_w, wT, oc, dbg=f"{proj}w{oc}")
                    proj_group(psP, b_psP, wsl, b_w,
                               lambda k: slabX[:, k * S:(k + 1) * S], bX, S, DC,
                               dbg=f"{proj}mm{oc}")
                    rec.emit(S_, lambda e: e.activation(
                        qs[:], psP[:, 0:S], ACT.Identity, bias=zcol),
                        reads=b_psP + [b_zc], writes=[b_qs], dbg=f"{proj}ev{oc}")
                    if proj in ("q", "k"):
                        items = [(lambda e, j=j: e.matmul(
                            psQ[:, j * 512:(j + 1) * 512], rotm[:],
                            qs[:, j * 512:(j + 1) * 512], start=True, stop=True),
                            [b_rotm, b_qs]) for j in range(4)]
                        rec.emit_group(T_, items, writes=b_psQ, dbg="rot")
                        rec.emit(S_, lambda e: e.activation(
                            t1f[:], psQ[:, 0:S], ACT.Identity, bias=zcol),
                            reads=b_psQ + [b_zc], writes=[b_t1], dbg="rtev")
                        rec.emit(V_, lambda e: e.tensor_mul(t2f[:], t1f[:], sinb[:]),
                                 reads=[b_t1, b_sin], writes=[b_t2], dbg="rp1")
                        rec.emit(V_, lambda e: e.tensor_mul(t1f[:], qs[:], cosb[:]),
                                 reads=[b_qs, b_cos], writes=[b_t1], dbg="rp2")
                        if proj == "k":
                            out_ap, b_out = slabK[:, oc * S:(oc + 1) * S], bK[oc]
                        else:
                            out_ap, b_out = qout[:], b_qout
                        rec.emit(V_, lambda e, o=out_ap: e.tensor_add(o, t1f[:], t2f[:]),
                                 reads=[b_t1, b_t2], writes=[b_out], dbg="rp3")
                        if proj == "q":
                            dma(G_, ch_sg, q_hbm[oc * 128:(oc + 1) * 128, :], qout[:],
                                [b_qout], [hb(q_hbm)], dbg=f"qst{oc}")
                    else:
                        rec.emit(S_, lambda e: e.activation(
                            qout[:], psP[:, 0:S], ACT.Identity, bias=zcol),
                            reads=b_psP + [b_zc], writes=[b_qout], dbg=f"vev{oc}")
                        dma(G_, ch_sg, v_hbm[oc * 128:(oc + 1) * 128, :], qout[:],
                            [b_qout], [hb(v_hbm)], dbg=f"vst{oc}")
            for k in range(DC):
                rec.emit(V_, lambda e, k=k: e.memset(slabX[:, k * S:(k + 1) * S], 0.0),
                         reads=[], writes=[bX[k]], dbg=f"zero{k}")
            if A_LIMIT == "p1only":
                continue
            rsf, b_rsf = TF[1], tf_b[1]
            for h in range(NH):
                pump_gamma(2)
                dma(SY, ch_ls, qh[:], q_hbm[h * 128:(h + 1) * 128, :], [hb(q_hbm)],
                    [b_qh], dbg=f"qh{h}")
                dma(SY, ch_ls, vf[:], v_hbm[h * 128:(h + 1) * 128, :], [hb(v_hbm)],
                    [b_vf], dbg=f"vf{h}")
                for g4 in range(4):
                    pq = psP[:, (g4 % 2) * 512:(g4 % 2) * 512 + 512]
                    pb = [b_psP[g4 % 2]]
                    items = [(lambda e, kk=kk, pq=pq, g4=g4: e.matmul(
                        pq[:, kk * 128:(kk + 1) * 128],
                        vf[:, (g4 * 4 + kk) * 128:(g4 * 4 + kk + 1) * 128],
                        ident[:], start=True, stop=True), [b_vf, b_ident])
                        for kk in range(4)]
                    rec.emit_group(T_, items, writes=pb, dbg="vtr")
                    rec.emit(S_, lambda e, g4=g4, pq=pq: e.activation(
                        vtok[:, g4 * 512:(g4 + 1) * 512], pq, ACT.Identity,
                        bias=zcol),
                        reads=pb + [b_zc], writes=[b_vtok], dbg="vtrev")
                for q4 in range(4):
                    q0 = q4 * 512
                    ktiles = q4 * 4 + 4
                    for kt in range(ktiles):
                        pq = psP[:, (kt % 2) * 512:(kt % 2) * 512 + 512]
                        pb = [b_psP[kt % 2]]
                        rec.emit_group(T_, [(lambda e, kt=kt, q0=q0, pq=pq, h=h:
                                             e.matmul(
                            pq, slabK[:, h * S + kt * 128:h * S + (kt + 1) * 128],
                            qh[:, q0:q0 + 512], start=True, stop=True),
                            [bK[h], b_qh])], writes=pb, dbg=f"sc{q4}.{kt}")
                        if kt >= q4 * 4:
                            off = (kt - q4 * 4) * 128
                            rec.emit(V_, lambda e, pq=pq, off=off: e.tensor_tensor(
                                pq[:, off:off + 128], pq[:, off:off + 128],
                                maskt[:], ALU.add),
                                reads=[b_maskt], writes=pb, dbg="maskT")
                            rec.emit(S_, lambda e, pq=pq, off=off, kt=kt, q0=q0:
                                     e.activation(
                                slabX[:, kt * S + q0 + off:kt * S + q0 + 512],
                                pq[:, off:512], ACT.Exp, bias=m20col, scale=ISQ),
                                reads=pb + [b_m20], writes=[bX[kt]], dbg="expb")
                        else:
                            rec.emit(S_, lambda e, pq=pq, kt=kt, q0=q0: e.activation(
                                slabX[:, kt * S + q0:kt * S + q0 + 512],
                                pq, ACT.Exp, bias=m20col, scale=ISQ),
                                reads=pb + [b_m20], writes=[bX[kt]], dbg="exp")
                        rec.emit_group(T_, [(lambda e, kt=kt, q0=q0, q4=q4,
                                             ktiles=ktiles: e.matmul(
                            psQ[:, q4 * 512:(q4 + 1) * 512], ones[:],
                            slabX[:, kt * S + q0:kt * S + q0 + 512],
                            start=(kt == 0), stop=(kt == ktiles - 1)),
                            [b_ones, bX[kt]])], writes=b_psQ, dbg=f"rs{q4}.{kt}")
                rec.emit(S_, lambda e: e.activation(rsf[:], psQ[:, 0:2048],
                                                    ACT.Identity, bias=zcol),
                         reads=b_psQ + [b_zc], writes=[b_rsf], dbg="rsev")
                rec.emit(V_, lambda e: e.reciprocal(rsf[:], rsf[:]),
                         reads=[], writes=[b_rsf], dbg="rsrcp")
                for kc in range(DC):
                    w0 = kc * 128
                    rec.emit(V_, lambda e, kc=kc, w0=w0: e.tensor_mul(
                        slabX[:, kc * S + w0:kc * S + S],
                        slabX[:, kc * S + w0:kc * S + S], rsf[:, w0:S]),
                        reads=[b_rsf], writes=[bX[kc]], dbg=f"nrm{kc}")
                for q4 in range(4):
                    q0 = q4 * 512
                    kmax = q4 * 4 + 4
                    pq = psP[:, (q4 % 2) * 512:(q4 % 2) * 512 + 512]
                    pb = [b_psP[q4 % 2]]
                    items = [(lambda e, kc=kc, pq=pq, q0=q0, kmax=kmax: e.matmul(
                        pq, vtok[:, kc * 128:(kc + 1) * 128],
                        slabX[:, kc * S + q0:kc * S + q0 + 512],
                        start=(kc == 0), stop=(kc == kmax - 1)),
                        [b_vtok, bX[kc]]) for kc in range(kmax)]
                    rec.emit_group(T_, items, writes=pb, dbg=f"av{q4}")
                    ao, b_ao = TB[7][:, (q4 % 2) * 512:(q4 % 2) * 512 + 512], tb_b[7]
                    rec.emit(S_, lambda e, ao=ao, pq=pq: e.activation(
                        ao, pq, ACT.Identity, bias=zcol),
                        reads=pb + [b_zc], writes=[b_ao], dbg="avev")
                    dma(G_, ch_sg, ao_hbm[h * 128:(h + 1) * 128, q4 * 512:(q4 + 1) * 512],
                        ao, [b_ao], [hb(ao_hbm)], dbg=f"aost{q4}")
            if A_LIMIT == "p2":
                continue
            for k in range(DC):
                dma(SY, ch_ls, slabK[:, k * S:(k + 1) * S],
                    ao_hbm[k * 128:(k + 1) * 128, :], [hb(ao_hbm)], [bK[k]],
                    dbg=f"aold{k}")
            proj_res_ln("A", woT, None,
                        lambda k: slabK[:, k * S:(k + 1) * S], bK, xT, resid1,
                        tok0, L, wsl2, dbg=f"wo{seq}")
        finish_all_gamma()
        rec.barrier("endA")

    def phase_b():
        AB32 = AB.bitcast(F32)
        slabXQ, bXQ = AA, bAA

        def abf(i):
            return AB32[:, i * 2048:(i + 1) * 2048], bAB[2 * i:2 * i + 2]
        xst = [abf(0), abf(1)]
        tmpa, b_tmpa = abf(2)
        tmpb, b_tmpb = abf(3)
        tmpc, b_tmpc = abf(4)
        xmax, b_xmax = abf(5)
        hmax, b_hmax = abf(6)
        sgg, b_sgg = abf(7)
        lt, b_lt = TB[0], tb_b[0]
        ternG, b_tG = TB[1], tb_b[1]
        ternV, b_tV = TB[2], tb_b[2]
        hstage, b_hst = TB[3], tb_b[3]
        owtern, b_owt = TB[4], tb_b[4]
        wfB = WFp
        wfO, b_wfO = TF[1], tf_b[1]

        gi_gw, gi_vw, gi_ow = GIDX["gw"], GIDX["vw"], GIDX["ow"]
        xmaxb, b_xmb = TB[5], tb_b[5]
        xts, b_xts = TB[6], tb_b[6]
        hmaxb, b_hmb = TB[7], tb_b[7]
        for g in range(2):
            tg0 = g * S
            for k in range(DC):
                xs, b_xs = xst[k % 2]
                dma(SY, ch_ls, xs, resid1[k * 128:(k + 1) * 128, tg0:tg0 + S],
                    [hb(resid1)], b_xs, dbg=f"qx{k}")
                if k == 0:
                    rec.emit(V_, lambda e, xs=xs: e.scalar_tensor_tensor(
                        xmaxb[:], xs, -1.0, xs, ALU.mult, ALU.max),
                        reads=b_xs, writes=[b_xmb], dbg="xm0")
                else:
                    rec.emit(V_, lambda e, xs=xs: e.scalar_tensor_tensor(
                        xts[:], xs, -1.0, xs, ALU.mult, ALU.max),
                        reads=b_xs, writes=[b_xts], dbg="xma")
                    rec.emit(V_, lambda e: e.tensor_max(xmaxb[:], xmaxb[:], xts[:]),
                             reads=[b_xts], writes=[b_xmb], dbg="xm")
            pmax_rows(xmaxb, b_xmb, xts, b_xts, rowm, b_rowm, dbg="xpm")
            rec.emit(V_, lambda e: e.reciprocal(rowr, rowm),
                     reads=[b_rowm], writes=[b_rowr])
            rec.emit(V_, lambda e: e.tensor_scalar_mul(rowr, rowr, 127.0),
                     reads=[], writes=[b_rowr])
            bcast_rows(rowr, b_rowr, sgg[:], b_sgg, 0, tg0, dbg="sbc")
            for k in range(DC):
                xs, b_xs = xst[k % 2]
                dma(SY, ch_ls, xs, resid1[k * 128:(k + 1) * 128, tg0:tg0 + S],
                    [hb(resid1)], b_xs, dbg=f"qx2{k}")
                rec.emit(V_, lambda e, xs=xs: e.tensor_mul(tmpa[:], xs, sgg[:]),
                         reads=b_xs + b_sgg, writes=b_tmpa, dbg="qm1")
                rec.emit(V_, lambda e: e.tensor_scalar_add(
                    tmpb[:], tmpa[:], 12582912.0),
                    reads=b_tmpa, writes=b_tmpb, dbg="qm2")
                rec.emit(V_, lambda e, k=k: e.tensor_scalar_add(
                    slabXQ[:, k * S:(k + 1) * S], tmpb[:], -12582912.0),
                    reads=b_tmpb, writes=[bXQ[k]], dbg="qm3")
            rec.emit(V_, lambda e: e.reciprocal(hmax[:], sgg[:]),
                     reads=b_sgg, writes=b_hmax, dbg="dq0")
            rec.emit(V_, lambda e: e.tensor_single_scalar(
                xmax[:], hmax[:], gsc[:, 2 * gi_vw:2 * gi_vw + 1], ALU.mult),
                reads=b_hmax + [b_gsc], writes=b_xmax, dbg="dqv")
            rec.emit(V_, lambda e: e.tensor_scalar_mul(xmax[:], xmax[:], 2.0),
                     reads=[], writes=b_xmax)
            rec.emit(V_, lambda e: e.tensor_single_scalar(
                sgg[:], hmax[:], gsc[:, 2 * gi_gw:2 * gi_gw + 1], ALU.mult),
                reads=b_hmax + [b_gsc], writes=b_sgg, dbg="dqg")
            rec.emit(V_, lambda e: e.tensor_scalar_mul(sgg[:], sgg[:], 2.0),
                     reads=[], writes=b_sgg)
            for hc in range(H0C):
                for mi, (mT, gi, tslab, b_ts, pp, ppb) in enumerate((
                        (gwT, gi_gw, ternG, b_tG, psP, b_psP),
                        (vwT, gi_vw, ternV, b_tV, psQ, b_psQ))):
                    wf, b_wf = wfB[mi]
                    dma(SY, ch_ls, wf,
                        mT[:, hc * 128:(hc + 1) * 128].rearrange("(n p) c -> p n c", p=128),
                        [hb(mT)], [b_wf], dbg=f"fw{hc}")
                    tern_tile(tslab[:], b_ts, wf, b_wf, lt, b_lt, gi, 2048,
                              dbg=f"tn{hc}")
                    proj_group(pp, ppb, tslab, b_ts,
                               lambda k: slabXQ[:, k * S:(k + 1) * S], bXQ, S, DC,
                               dbg=f"ff{hc}")
                gbf, b_gbf = TB[5], tb_b[5]
                vbf, b_vbf = TB[6], tb_b[6]
                rec.emit(S_, lambda e: e.activation(
                    gbf[:], psP[:, 0:S], ACT.Identity, bias=zcol),
                    reads=b_psP + [b_zc], writes=[b_gbf], dbg="evgc")
                rec.emit(S_, lambda e: e.activation(
                    vbf[:], psQ[:, 0:S], ACT.Identity, bias=zcol),
                    reads=b_psQ + [b_zc], writes=[b_vbf], dbg="evvc")
                rec.emit(V_, lambda e: e.tensor_mul(tmpa[:], gbf[:], sgg[:]),
                         reads=[b_gbf] + b_sgg, writes=b_tmpa, dbg="evg")
                rec.emit(S_, lambda e, hc=hc: e.activation(
                    tmpb[:], tmpa[:], ACT.Silu, bias=gbc[:, hc:hc + 1]),
                    reads=b_tmpa + [b_gbc], writes=b_tmpb, dbg="silu")
                rec.emit(V_, lambda e: e.tensor_mul(tmpc[:], vbf[:], xmax[:]),
                         reads=[b_vbf] + b_xmax, writes=b_tmpc, dbg="evv")
                rec.emit(V_, lambda e, hc=hc: e.scalar_tensor_tensor(
                    hstage[:], tmpc[:], vbc[:, hc:hc + 1], tmpb[:], ALU.add, ALU.mult),
                    reads=b_tmpc + [b_vbc] + b_tmpb, writes=[b_hst], dbg="hmul")
                if hc == 0:
                    rec.emit(V_, lambda e: e.scalar_tensor_tensor(
                        hmaxb[:], hstage[:], -1.0, hstage[:], ALU.mult, ALU.max),
                        reads=[b_hst], writes=[b_hmb], dbg="hm0")
                else:
                    rec.emit(V_, lambda e: e.scalar_tensor_tensor(
                        tmpa[:], hstage[:], -1.0, hstage[:], ALU.mult, ALU.max),
                        reads=[b_hst], writes=b_tmpa, dbg="hma")
                    rec.emit(V_, lambda e: e.tensor_max(hmaxb[:], hmaxb[:], tmpa[:]),
                             reads=b_tmpa, writes=[b_hmb], dbg="hm")
                dma(G_, ch_sg, hmul_hbm[hc * 128:(hc + 1) * 128, tg0:tg0 + S],
                    hstage[:], [b_hst], [hb(hmul_hbm)], dbg=f"hst{hc}")
                if g == 0:
                    hk = hc
                    dma(SY, ch_ls, wfO[:], owT[hk * 128:(hk + 1) * 128, :],
                        [hb(owT)], [b_wfO], dbg=f"owl{hk}")
                    tern_tile(owtern[:], b_owt, wfO[:], b_wfO, lt, b_lt, gi_ow, 2048,
                              dbg=f"owt{hk}")
                    dma(G_, ch_sg,
                        town[hk * 8:(hk + 1) * 8, :].rearrange("r (p c) -> p r c", p=128),
                        owtern[:], [b_owt], [hb(town)], dbg=f"ows{hk}")
            pmax_rows(hmaxb, b_hmb, xts, b_xts, rowm, b_rowm, dbg="hpm")
            rec.emit(V_, lambda e: e.reciprocal(rowr, rowm),
                     reads=[b_rowm], writes=[b_rowr])
            rec.emit(V_, lambda e: e.tensor_scalar_mul(rowr, rowr, 127.0),
                     reads=[], writes=[b_rowr])
            dma(G_, ch_sg,
                srow_hbm[0:1, tg0:tg0 + S].rearrange("o (r c) -> (o r) c", r=16),
                rowr, [b_rowr], [hb(srow_hbm)], dbg="srow")

        rec.barrier("endB1")

        def hqs(hk):
            if hk < 32:
                return AA[:, hk * 1024:(hk + 1) * 1024], bAA[hk // 2]
            return AB[:, (hk - 32) * 1024:(hk - 31) * 1024], bAB[(hk - 32) // 2]
        wsl2 = [(TB[0], tb_b[0]), (TB[1], tb_b[1])]
        qy, b_qy = TF[0][:, 0:1024], tf_b[0]
        yf, b_yf = TF[0][:, 1024:2048], tf_b[0]
        qm, b_qm = yf, b_yf
        s2b, b_s2b = TF[1][:, 0:1024], tf_b[1]
        gsob, b_gsob = TF[1][:, 1024:2048], tf_b[1]
        L = LN

        for tt in range(4):
            u0 = tt * 1024
            dma(G_, ch_lg, s2b,
                srow_hbm[0:1, u0:u0 + 1024].broadcast_to((128, 1024)),
                [hb(srow_hbm)], [b_s2b], dbg="sbc")
            for hk in range(H0C):
                hm = TB[2][:, (hk % 2) * 1024:(hk % 2) * 1024 + 1024]
                b_h = tb_b[2]
                dma(SY, ch_ls, hm, hmul_hbm[hk * 128:(hk + 1) * 128, u0:u0 + 1024],
                    [hb(hmul_hbm)], [b_h], dbg=f"hml{hk}")
                rec.emit(V_, lambda e, hm=hm: e.tensor_mul(qy, hm, s2b),
                         reads=[b_h, b_s2b], writes=[b_qy], dbg="hq1")
                rec.emit(V_, lambda e: e.tensor_scalar_add(qm, qy, 12582912.0),
                         reads=[b_qy], writes=[b_qm], dbg="hq2")
                hq, b_hq = hqs(hk)
                rec.emit(V_, lambda e, hq=hq: e.tensor_scalar_add(hq, qm, -12582912.0),
                         reads=[b_qm], writes=[b_hq], dbg="hq3")
            gi_ow2 = GIDX["ow"]
            rec.emit(V_, lambda e: e.reciprocal(gsob, s2b),
                     reads=[b_s2b], writes=[b_gsob], dbg="gbc0")
            rec.emit(V_, lambda e, gi=gi_ow2: e.tensor_single_scalar(
                gsob, gsob, gsc[:, 2 * gi:2 * gi + 1], ALU.mult),
                reads=[b_gsc], writes=[b_gsob], dbg="gbc1")
            rec.emit(V_, lambda e: e.tensor_scalar_mul(gsob, gsob, 2.0),
                     reads=[], writes=[b_gsob], dbg="gbc2")
            for ocp in range(8):
                for i in range(H0C // 8):
                    wsl, b_w = wsl2[i % 2]
                    dma(SY, ch_ls, wsl,
                        town[ocp + i * 64:ocp + i * 64 + 57:8, :].rearrange(
                            "r (p c) -> p r c", p=128),
                        [hb(town)], [b_w], dbg=f"twl{ocp}.{i}")
                    items = []
                    for k in range(8):
                        hk = i * 8 + k
                        hq, b_hq = hqs(hk)
                        for half in range(2):
                            for j in range(2):
                                items.append((lambda e, k=k, hq=hq, half=half, j=j,
                                              wsl=wsl, hk=hk: e.matmul(
                                    psP[:, half * 1024 + j * 512:half * 1024 + (j + 1) * 512],
                                    wsl[:, k * 256 + half * 128:k * 256 + half * 128 + 128],
                                    hq[:, j * 512:(j + 1) * 512],
                                    start=(hk == 0), stop=(hk == H0C - 1)),
                                    [b_w, b_hq]))
                    rec.emit_group(T_, items, writes=b_psP, dbg=f"op{ocp}.{i}")
                for half in range(2):
                    oc = ocp * 2 + half
                    rx, b_rx = L["rxs"][oc % 2]
                    dma(SY, ch_ls, rx, resid1[oc * 128:(oc + 1) * 128, u0:u0 + 1024],
                        [hb(resid1)], [b_rx], dbg=f"brx{oc}")
                    ybf, b_ybf = TB[4][:, 0:1024], tb_b[4]
                    rec.emit(S_, lambda e, half=half, ybf=ybf: e.activation(
                        ybf, psP[:, half * 1024:half * 1024 + 1024], ACT.Identity,
                        bias=zcol),
                        reads=b_psP[half * 2:half * 2 + 2] + [b_zc],
                        writes=[b_ybf], dbg="bev0")
                    rec.emit(V_, lambda e, ybf=ybf: e.tensor_mul(yf, ybf, gsob),
                             reads=[b_ybf, b_gsob], writes=[b_yf], dbg="bev1")
                    yst, b_yst = L["ystage"][oc % 2]
                    rec.emit(V_, lambda e, oc=oc, rx=rx, yst=yst:
                             e.scalar_tensor_tensor(
                        yst, yf, obc[:, oc:oc + 1], rx, ALU.add, ALU.add),
                        reads=[b_yf, b_obc, b_rx], writes=[b_yst], dbg="bev2")
                    dma(G_, ch_sg, y_scr[oc * 128:(oc + 1) * 128, u0:u0 + 1024],
                        yst, [b_yst], [hb(y_scr)], dbg=f"bys{oc}")
                    ln_fold(L, oc, oc == 0, oc == DC - 1, u0, dbg="bln")
            ln_finish(L, y_scr, resid2, u0, True, dbg=f"blnf{tt}")
        rec.barrier("endB2")

    def phase_c():
        slabX, bX = AA, bAA
        slabQ, bQ = AB, bAB
        ktile, b_kt = TB[2], tb_b[2]
        vtile, b_vt = TB[3], tb_b[3]
        vtok, b_vtok = TB[4], tb_b[4]
        wsl2 = [(TB[0], tb_b[0]), (TB[1], tb_b[1])]
        lt, b_lt = TB[7][:, 0:1024], tb_b[7]
        w3t, b_w3t = TB[7][:, 1024:2048], tb_b[7]
        L = LN
        gi_w3 = GIDX["w3"]
        w3_next = [0]

        def pump_w3(n=1):
            for _ in range(n):
                step = w3_next[0]
                if step >= 2 * H1C:
                    return
                w3_next[0] += 1
                hk, sub = step // 2, step % 2
                wf, b_wf = WFp[step % 2]
                wf = wf[:, 0:1024]
                dma(SY, ch_ls, wf, w3T[hk * 128:(hk + 1) * 128,
                                       sub * 1024:(sub + 1) * 1024],
                    [hb(w3T)], [b_wf], dbg=f"w3l{step}")
                tern_tile(w3t[:], b_w3t, wf, b_wf, lt, b_lt, gi_w3, 1024,
                          dbg=f"w3t{step}")
                dma(G_, ch_sg,
                    tw3[hk * 8 + sub * 4:hk * 8 + sub * 4 + 4, :].rearrange(
                        "r (p c) -> p r c", p=128),
                    w3t[:], [b_w3t], [hb(tw3)], dbg=f"w3s{step}")

        for seq in range(2):
            tok0 = seq * S
            for k in range(DC):
                dma(G_, ch_lg, slabX[:, k * S:(k + 1) * S],
                    resid2[k * 128:(k + 1) * 128, tok0:tok0 + S],
                    [hb(resid2)], [bX[k]], dbg=f"cx{k}")
            for oc in range(18):
                pump_w3(1)
                wsl, b_w = wsl2[oc % 2]
                wsl_load(wsl, b_w, wqkv, oc, dbg=f"cqkvw{oc}")
                proj_group(psP, b_psP, wsl, b_w,
                           lambda k: slabX[:, k * S:(k + 1) * S], bX, S, DC,
                           dbg=f"cqkv{oc}")
                if oc < 16:
                    rec.emit(S_, lambda e, oc=oc: e.activation(
                        slabQ[:, oc * S:(oc + 1) * S], psP[:, 0:S], ACT.Identity,
                        bias=qkvbc[:, oc:oc + 1]),
                        reads=b_psP + [b_qkvbc], writes=[bQ[oc]], dbg=f"cqe{oc}")
                elif oc == 16:
                    rec.emit(S_, lambda e: e.activation(
                        ktile[:], psP[:, 0:S], ACT.Identity, bias=zcol),
                        reads=b_psP + [b_zc], writes=[b_kt], dbg="cke")
                else:
                    rec.emit(S_, lambda e: e.activation(
                        vtile[:], psP[:, 0:S], ACT.Identity, bias=zcol),
                        reads=b_psP + [b_zc], writes=[b_vt], dbg="cve")
            for g4 in range(4):
                pq = psP[:, (g4 % 2) * 512:(g4 % 2) * 512 + 512]
                pb = [b_psP[g4 % 2]]
                items = [(lambda e, kk=kk, pq=pq, g4=g4: e.matmul(
                    pq[:, kk * 128:(kk + 1) * 128],
                    vtile[:, (g4 * 4 + kk) * 128:(g4 * 4 + kk + 1) * 128],
                    ident[:], start=True, stop=True), [b_vt, b_ident])
                    for kk in range(4)]
                rec.emit_group(T_, items, writes=pb, dbg="cvtr")
                rec.emit(S_, lambda e, g4=g4, pq=pq: e.activation(
                    vtok[:, g4 * 512:(g4 + 1) * 512], pq, ACT.Identity, bias=zcol),
                    reads=pb + [b_zc], writes=[b_vtok], dbg="cvtrev")
            for k in range(DC):
                rec.emit(V_, lambda e, k=k: e.memset(slabX[:, k * S:(k + 1) * S], 0.0),
                         reads=[], writes=[bX[k]], dbg=f"czero{k}")
            rsf, b_rsf = TF[1], tf_b[1]
            for h in range(NH):
                pump_w3(1)
                for q4 in range(4):
                    q0 = q4 * 512
                    ktiles = q4 * 4 + 4
                    for kt in range(ktiles):
                        pq = psP[:, (kt % 2) * 512:(kt % 2) * 512 + 512]
                        pb = [b_psP[kt % 2]]
                        rec.emit_group(T_, [(lambda e, kt=kt, q0=q0, pq=pq, h=h:
                                             e.matmul(
                            pq, ktile[:, kt * 128:(kt + 1) * 128],
                            slabQ[:, h * S + q0:h * S + q0 + 512],
                            start=True, stop=True),
                            [b_kt, bQ[h]])], writes=pb, dbg=f"csc{q4}.{kt}")
                        if kt >= q4 * 4:
                            off = (kt - q4 * 4) * 128
                            rec.emit(V_, lambda e, pq=pq, off=off: e.tensor_tensor(
                                pq[:, off:off + 128], pq[:, off:off + 128],
                                maskt[:], ALU.add),
                                reads=[b_maskt], writes=pb, dbg="cmaskT")
                            rec.emit(S_, lambda e, pq=pq, off=off, kt=kt, q0=q0:
                                     e.activation(
                                slabX[:, kt * S + q0 + off:kt * S + q0 + 512],
                                pq[:, off:512], ACT.Exp, bias=m20col, scale=ISQ),
                                reads=pb + [b_m20], writes=[bX[kt]], dbg="cexpb")
                        else:
                            rec.emit(S_, lambda e, pq=pq, kt=kt, q0=q0: e.activation(
                                slabX[:, kt * S + q0:kt * S + q0 + 512],
                                pq, ACT.Exp, bias=m20col, scale=ISQ),
                                reads=pb + [b_m20], writes=[bX[kt]], dbg="cexp")
                        rec.emit_group(T_, [(lambda e, kt=kt, q0=q0, q4=q4,
                                             ktiles=ktiles: e.matmul(
                            psQ[:, q4 * 512:(q4 + 1) * 512], ones[:],
                            slabX[:, kt * S + q0:kt * S + q0 + 512],
                            start=(kt == 0), stop=(kt == ktiles - 1)),
                            [b_ones, bX[kt]])], writes=b_psQ, dbg=f"crs{q4}.{kt}")
                rec.emit(S_, lambda e: e.activation(rsf[:], psQ[:, 0:2048],
                                                    ACT.Identity, bias=zcol),
                         reads=b_psQ + [b_zc], writes=[b_rsf], dbg="crsev")
                rec.emit(V_, lambda e: e.reciprocal(rsf[:], rsf[:]),
                         reads=[], writes=[b_rsf], dbg="crsrcp")
                for kc in range(DC):
                    w0 = kc * 128
                    rec.emit(V_, lambda e, kc=kc, w0=w0: e.tensor_mul(
                        slabX[:, kc * S + w0:kc * S + S],
                        slabX[:, kc * S + w0:kc * S + S], rsf[:, w0:S]),
                        reads=[b_rsf], writes=[bX[kc]], dbg=f"cnrm{kc}")
                for q4 in range(4):
                    q0 = q4 * 512
                    kmax = q4 * 4 + 4
                    pq = psP[:, (q4 % 2) * 512:(q4 % 2) * 512 + 512]
                    pb = [b_psP[q4 % 2]]
                    items = [(lambda e, kc=kc, pq=pq, q0=q0, kmax=kmax: e.matmul(
                        pq, vtok[:, kc * 128:(kc + 1) * 128],
                        slabX[:, kc * S + q0:kc * S + q0 + 512],
                        start=(kc == 0), stop=(kc == kmax - 1)),
                        [b_vtok, bX[kc]]) for kc in range(kmax)]
                    rec.emit_group(T_, items, writes=pb, dbg=f"cav{q4}")
                    rec.emit(S_, lambda e, h=h, q4=q4, pq=pq: e.activation(
                        slabQ[:, h * S + q4 * 512:h * S + (q4 + 1) * 512],
                        pq, ACT.Identity, bias=qkvbc[:, 17:18]),
                        reads=pb + [b_qkvbc], writes=[bQ[h]], dbg="cavev")
            proj_res_ln("C", wfc, lambda oc: (fcbc[:, oc:oc + 1], b_fcbc),
                        lambda k: slabQ[:, k * S:(k + 1) * S], bQ, resid2, resid3,
                        tok0, L, wsl2, dbg=f"cfc{seq}")
        while w3_next[0] < 2 * H1C:
            pump_w3(4)
        rec.barrier("endC")

    def phase_d():
        slabX, bX = AA, bAA
        lt, b_lt = TB[0], tb_b[0]
        ternG, b_tG = TB[1], tb_b[1]
        ternV, b_tV = TB[2], tb_b[2]
        sig, b_sig = TF[3], tf_b[3]
        hstage, b_hst = TB[3], tb_b[3]
        wfB = WFp
        gi_w1, gi_w2 = GIDX["w1"], GIDX["w2"]
        for g in range(2):
            tg0 = g * S
            for k in range(DC):
                dma(G_, ch_lg, slabX[:, k * S:(k + 1) * S],
                    resid3[k * 128:(k + 1) * 128, tg0:tg0 + S],
                    [hb(resid3)], [bX[k]], dbg=f"dx{k}")
            for hc in range(H1C):
                for mi, (mT, gi, tslab, b_ts, pp, ppb) in enumerate((
                        (w1T, gi_w1, ternG, b_tG, psP, b_psP),
                        (w2T, gi_w2, ternV, b_tV, psQ, b_psQ))):
                    wf, b_wf = wfB[mi]
                    dma(SY, ch_ls, wf,
                        mT[:, hc * 128:(hc + 1) * 128].rearrange("(n p) c -> p n c", p=128),
                        [hb(mT)], [b_wf], dbg=f"dw{hc}")
                    tern_tile(tslab[:], b_ts, wf, b_wf, lt, b_lt, gi, 2048,
                              dbg=f"dt{hc}")
                    proj_group(pp, ppb, tslab, b_ts,
                               lambda k: slabX[:, k * S:(k + 1) * S], bX, S, DC,
                               dbg=f"dff{hc}")
                x2bf, b_x2 = TB[5], tb_b[5]
                rec.emit(S_, lambda e: e.activation(sig[:], psP[:, 0:S], ACT.Sigmoid),
                         reads=b_psP, writes=[b_sig], dbg="dsig")
                rec.emit(S_, lambda e: e.activation(
                    x2bf[:], psQ[:, 0:S], ACT.Identity, bias=zcol),
                    reads=b_psQ + [b_zc], writes=[b_x2], dbg="dx2")
                rec.emit(V_, lambda e: e.tensor_mul(hstage[:], sig[:], x2bf[:]),
                         reads=[b_sig, b_x2], writes=[b_hst], dbg="dh")
                dma(G_, ch_sg, h1_hbm[hc * 128:(hc + 1) * 128, tg0:tg0 + S],
                    hstage[:], [b_hst], [hb(h1_hbm)], dbg=f"dhs{hc}")
        rec.barrier("endD1")

        def hsl(hk):
            if hk < 32:
                return AA[:, hk * 1024:(hk + 1) * 1024], bAA[hk // 2]
            return AB[:, (hk - 32) * 1024:(hk - 31) * 1024], bAB[(hk - 32) // 2]
        wsl2 = [(TB[0], tb_b[0]), (TB[1], tb_b[1])]
        L = LN
        for tt in range(4):
            u0 = tt * 1024
            for hk in range(H1C):
                hq, b_hq = hsl(hk)
                dma(SY, ch_ls, hq,
                    h1_hbm[hk * 128:(hk + 1) * 128, u0:u0 + 1024],
                    [hb(h1_hbm)], [b_hq], dbg=f"dhl{hk}")
            for ocp in range(8):
                for i in range((H1C + 7) // 8):
                    r0, r1 = i * 8, min((i + 1) * 8, H1C)
                    nr = r1 - r0
                    wsl, b_w = wsl2[i % 2]
                    dma(SY, ch_ls, wsl[:, 0:nr * 256],
                        tw3[ocp + r0 * 8:ocp + (r1 - 1) * 8 + 1:8, :].rearrange(
                            "r (p c) -> p r c", p=128),
                        [hb(tw3)], [b_w], dbg=f"dwl{ocp}.{i}")
                    items = []
                    for k in range(nr):
                        hk = r0 + k
                        hq, b_hq = hsl(hk)
                        for half in range(2):
                            for j in range(2):
                                items.append((lambda e, k=k, hq=hq, half=half, j=j,
                                              wsl=wsl, hk=hk: e.matmul(
                                    psP[:, half * 1024 + j * 512:half * 1024 + (j + 1) * 512],
                                    wsl[:, k * 256 + half * 128:k * 256 + half * 128 + 128],
                                    hq[:, j * 512:(j + 1) * 512],
                                    start=(hk == 0), stop=(hk == H1C - 1)),
                                    [b_w, b_hq]))
                    rec.emit_group(T_, items, writes=b_psP, dbg=f"dop{ocp}.{i}")
                for half in range(2):
                    oc = ocp * 2 + half
                    rx, b_rx = L["rxs"][oc % 2]
                    dma(SY, ch_ls, rx, resid3[oc * 128:(oc + 1) * 128, u0:u0 + 1024],
                        [hb(resid3)], [b_rx], dbg=f"drx{oc}")
                    yst, b_yst = L["ystage"][oc % 2]
                    rec.emit(S_, lambda e, half=half, yst=yst: e.activation(
                        yst, psP[:, half * 1024:half * 1024 + 1024], ACT.Identity,
                        bias=zcol),
                        reads=b_psP[half * 2:half * 2 + 2] + [b_zc],
                        writes=[b_yst], dbg="dev0")
                    rec.emit(V_, lambda e, rx=rx, yst=yst: e.tensor_add(
                        yst, yst, rx),
                        reads=[b_rx], writes=[b_yst], dbg="dev")
                    dma(G_, ch_sg, y_scr[oc * 128:(oc + 1) * 128, u0:u0 + 1024],
                        yst, [b_yst], [hb(y_scr)], dbg=f"dys{oc}")
                    ln_fold(L, oc, oc == 0, oc == DC - 1, u0, dbg="dln")
            ln_finish(L, y_scr, yT, u0, True, dbg=f"dlnf{tt}")
        rec.barrier("endD2")

    if "A" in phases:
        phase_a()
    if "B" in phases:
        phase_b()
    if "C" in phases:
        phase_c()
    if "D" in phases:
        phase_d()

    if debug_tap is not None:
        srct = {"r1": resid1, "r2": resid2, "r3": resid3}[debug_tap]
        for k in range(DC):
            for half in range(2):
                tap, b_tap = TF[half], tf_b[half]
                dma(SY, ch_ls, tap[:], srct[k * 128:(k + 1) * 128,
                                            half * 2048:(half + 1) * 2048],
                    [hb(srct)], [b_tap])
                dma(G_, ch_sg, yT[k * 128:(k + 1) * 128,
                                  half * 2048:(half + 1) * 2048],
                    tap[:], [b_tap], [hb(yT)])
        rec.barrier("tap")

    rec.barrier("final")
    rec.replay()
    stack.close()
    return nc


def _host_tables():
    HD = 128
    inv_freq = 1.0 / (10000.0 ** (np.arange(0, HD, 2, dtype=np.float32) / HD))
    t = np.arange(S, dtype=np.float32)
    freqs = np.einsum("i,j->ij", t, inv_freq)
    emb = np.concatenate((freqs, freqs), axis=-1)
    cos = np.cos(emb).T.astype(ml_dtypes.bfloat16)
    sin = np.sin(emb).T.astype(ml_dtypes.bfloat16)
    ident = np.eye(128, dtype=ml_dtypes.bfloat16)
    ones = np.ones((128, 128), dtype=ml_dtypes.bfloat16)
    rot = np.zeros((128, 128), dtype=np.float32)
    for i in range(64):
        rot[i + 64, i] = -1.0
    for i in range(64, 128):
        rot[i - 64, i] = 1.0
    rot = rot.astype(ml_dtypes.bfloat16)
    mask = np.tril(np.full((128, 128), -1e9, dtype=np.float32), k=-1)
    return cos, sin, ident, ones, rot, mask


def kernel(L0_attn_hidden_states, L0_attn_Wq, L0_attn_Wk, L0_attn_Wv, L0_attn_Wo,
           L0_ffn_gate_w, L0_ffn_gate_b, L0_ffn_val_w, L0_ffn_val_b,
           L0_ffn_out_w, L0_ffn_out_b, L1_attn_w_qkv, L1_attn_b_qkv,
           L1_attn_w_fc, L1_attn_b_fc, L1_ffn_w1, L1_ffn_w2, L1_ffn_w3):
    x = np.asarray(L0_attn_hidden_states, dtype=np.float32)
    B = x.shape[0]
    n_cores = 8
    per = B // n_cores

    cos, sin, ident, ones, rot, mask = _host_tables()
    AS = np.ascontiguousarray
    shared = {
        "wqT": AS(np.asarray(L0_attn_Wq, np.float32).T),
        "wkT": AS(np.asarray(L0_attn_Wk, np.float32).T),
        "wvT": AS(np.asarray(L0_attn_Wv, np.float32).T),
        "woT": AS(np.asarray(L0_attn_Wo, np.float32).T),
        "gwT": AS(np.asarray(L0_ffn_gate_w, np.float32).T),
        "vwT": AS(np.asarray(L0_ffn_val_w, np.float32).T),
        "owT": AS(np.asarray(L0_ffn_out_w, np.float32).T),
        "wqkv": AS(np.asarray(L1_attn_w_qkv, np.float32)),
        "wfc": AS(np.asarray(L1_attn_w_fc, np.float32)),
        "w1T": AS(np.asarray(L1_ffn_w1, np.float32).T),
        "w2T": AS(np.asarray(L1_ffn_w2, np.float32).T),
        "w3T": AS(np.asarray(L1_ffn_w3, np.float32).T),
        "gbH": AS(np.asarray(L0_ffn_gate_b, np.float32).reshape(H0C, 128).T),
        "vbH": AS(np.asarray(L0_ffn_val_b, np.float32).reshape(H0C, 128).T),
        "obH": AS(np.asarray(L0_ffn_out_b, np.float32).reshape(DC, 128).T),
        "qkvbH": AS(np.asarray(L1_attn_b_qkv, np.float32).reshape(18, 128).T),
        "fcbH": AS(np.asarray(L1_attn_b_fc, np.float32).reshape(DC, 128).T),
        "cosH": cos, "sinH": sin, "identH": ident, "onesH": ones,
        "rotH": rot, "maskH": mask,
    }
    in_maps = []
    for c in range(n_cores):
        xs = x[c * per:(c + 1) * per]
        xTc = AS(xs.transpose(2, 0, 1).reshape(D, per * S))
        m = dict(shared)
        m["xT"] = xTc
        in_maps.append(m)

    nc = build_program()
    res = run_bass_kernel_spmd(nc, in_maps, core_ids=list(range(n_cores)))
    outs = []
    for c in range(n_cores):
        yTc = np.asarray(res.results[c]["yT"])
        outs.append(yTc.reshape(D, per, S).transpose(1, 2, 0))
    return np.concatenate(outs, axis=0).astype(np.float32)


if __name__ == "__main__":
    pass
